# version 06
# baseline (speedup 1.0000x reference)
import numpy as np
import ml_dtypes
import concourse.bass as bass
import concourse.mybir as mybir
from concourse.bass_utils import run_bass_kernel_spmd

F32 = mybir.dt.float32
BF16 = mybir.dt.bfloat16
I32 = mybir.dt.int32
AF = mybir.ActivationFunctionType
ALU = mybir.AluOpType
AX = mybir.AxisListType

D = 4096
L = 4096
NH = 16
HD = 256
NFB = 32
HALF = 512
SCALE = 1.0 / 16.0
LN_EPS = 1e-5
FFN_EPS = 1e-5
KB = 1024
PARANOID = False

ENGS = ("tensor", "scalar", "vector", "gpsimd", "sync")


def _build_tables():
    g = [1.0, 1.0, 1.0]
    n = 8
    masks = sorted(range(n), key=lambda m: (bin(m).count("1"), m))
    inv = {m: i for i, m in enumerate(masks)}
    C = np.zeros((n, n, n), dtype=np.float32)
    for i, a in enumerate(masks):
        for j, b in enumerate(masks):
            sw, at = 0, a >> 1
            while at:
                sw += bin(at & b).count("1")
                at >>= 1
            s = -1.0 if sw & 1 else 1.0
            for t in range(3):
                if a >> t & 1 and b >> t & 1:
                    s *= float(g[t])
            C[i, j, inv[a ^ b]] = s
    grades = np.array([bin(m).count("1") for m in masks])
    rev = np.where(grades * (grades - 1) // 2 % 2 == 1, -1.0, 1.0).astype(np.float32)
    met = np.array(
        [float(np.prod([g[t] for t in range(3) if m >> t & 1] + [1.0])) for m in masks],
        dtype=np.float32,
    )
    return C, rev, met


def _host_consts():
    C, rev, met = _build_tables()
    biv = C[:, :, [4, 5, 6]]
    d0 = np.zeros((128, 128), np.float32)
    for p in range(128):
        d0[p, p] = rev[p % 8] * met[p % 8] * SCALE
    G = np.zeros((3, 128, 128), np.float32)
    for c in range(3):
        blk = rev[:, None] * biv[:, :, c]
        for dd in range(16):
            G[c, dd * 8:(dd + 1) * 8, dd * 8:(dd + 1) * 8] = blk
    d0g = np.concatenate([d0, G[0], G[1], G[2]], axis=1)
    ident = np.eye(128, dtype=np.float32)
    ones = np.ones((128, 128), np.float32)
    return dict(
        d0g=d0g,
        ident_f32=ident,
        ident_bf16=ident.astype(ml_dtypes.bfloat16),
        ones_f32=ones,
        ones_bf16=ones.astype(ml_dtypes.bfloat16),
    )


def _swizzle_w(wT):
    return np.ascontiguousarray(wT.reshape(32, 128, 16, 256).transpose(2, 1, 0, 3))


def _swizzle_x(xT, perm):
    xs = xT.reshape(32, 128, 8, 512).transpose(2, 1, 0, 3)
    return np.ascontiguousarray(xs[perm])


class PSem:
    __slots__ = ("name", "n", "h")

    def __init__(self, name):
        self.name, self.n, self.h = name, 0, None


class Plan:
    def __init__(self):
        self.q = {e: [] for e in ENGS}
        self.wcache = {}

    def emit(self, e, fn):
        self.q[e].append(fn)

    def wait(self, e, sem, val):
        if val <= 0:
            return
        k = (e, sem.name)
        if self.wcache.get(k, -1) >= val:
            return
        self.wcache[k] = val
        self.emit(e, lambda eng, s=sem, v=val: eng.wait_ge(s.h, v))


def build_program(dbg=False, upto="full"):
    nc = bass.Bass("TRN2", target_bir_lowering=False)

    x_sw = nc.dram_tensor("x_sw", [8, 128, NFB, 512], F32, kind="ExternalInput").ap()
    wq_sw = nc.dram_tensor("wq_sw", [16, 128, 32, 256], F32, kind="ExternalInput").ap()
    wk_sw = nc.dram_tensor("wk_sw", [16, 128, 32, 256], F32, kind="ExternalInput").ap()
    wv_sw = nc.dram_tensor("wv_sw", [16, 128, 32, 256], F32, kind="ExternalInput").ap()
    wf_sw = nc.dram_tensor("wf_sw", [16, 128, 32, 256], I32, kind="ExternalInput").ap()
    bq_in = nc.dram_tensor("bq_r", [128, 32], F32, kind="ExternalInput").ap()
    bk_in = nc.dram_tensor("bk_r", [128, 32], F32, kind="ExternalInput").ap()
    bv_in = nc.dram_tensor("bv_r", [128, 32], F32, kind="ExternalInput").ap()
    bb_in = nc.dram_tensor("bias_r", [128, 32], F32, kind="ExternalInput").ap()
    bw_in = nc.dram_tensor("bw_flat", [1, 48], F32, kind="ExternalInput").ap()
    ws_in = nc.dram_tensor("wscale", [1, 1], F32, kind="ExternalInput").ap()
    d0g_in = nc.dram_tensor("d0g", [128, 512], F32, kind="ExternalInput").ap()
    idf_in = nc.dram_tensor("ident_f32", [128, 128], F32, kind="ExternalInput").ap()
    idb_in = nc.dram_tensor("ident_bf16", [128, 128], BF16, kind="ExternalInput").ap()
    onf_in = nc.dram_tensor("ones_f32", [128, 128], F32, kind="ExternalInput").ap()
    onb_in = nc.dram_tensor("ones_bf16", [128, 128], BF16, kind="ExternalInput").ap()

    outT = nc.dram_tensor("outT", [2, 128, NFB, 512], F32, kind="ExternalOutput").ap()

    xbd = nc.dram_tensor("xbd", [8, 128, NFB, 512], BF16, kind="Internal").ap()
    kd = nc.dram_tensor("kd", [32, 128, L], BF16, kind="Internal").ap()
    vd = nc.dram_tensor("vd", [NH, 32, 128, HD], BF16, kind="Internal").ap()

    dbg_outs = {}
    if dbg:
        def dbgt(name, shape, dt):
            dbg_outs[name] = nc.dram_tensor(name, shape, dt, kind="ExternalOutput").ap()
        dbgt("dbg_xbd", [8, 128, NFB, 512], BF16)
        dbgt("dbg_k", [32, 128, L], BF16)
        dbgt("dbg_v", [NH, 32, 128, HD], BF16)
        dbgt("dbg_qeff", [2, 128, 32 * 512], BF16)
        dbgt("dbg_attn", [2, 128, NFB * 512], F32)
        dbgt("dbg_ln1", [2, 128, NFB * 512], F32)
        dbgt("dbg_ffn", [2, 128, NFB * 512], F32)
        dbgt("dbg_bd", [128, 16 * 128], BF16)

    BASE0 = (nc._sbuf_addr_for_side("left") + 63) // 64 * 64

    def at(name, shape, dtype, off):
        return nc.alloc_sbuf_tensor_at(name, shape, dtype, offset=BASE0 + off).ap()

    o = [0]

    def nxt(sz):
        r = o[0]
        o[0] += (sz + 31) // 32 * 32
        return r

    ident_f32 = at("c_idf", [128, 128], F32, nxt(512))
    ident_bf16 = at("c_idb", [128, 128], BF16, nxt(256))
    ones_f32 = at("c_onf", [128, 128], F32, nxt(512))
    ones_bf16 = at("c_onb", [128, 128], BF16, nxt(256))
    d0g_sb = at("c_d0g", [128, 512], F32, nxt(2048))
    bq_sb = at("c_bq", [128, 32], F32, nxt(128))
    bk_sb = at("c_bk", [128, 32], F32, nxt(128))
    bv_sb = at("c_bv", [128, 32], F32, nxt(128))
    bb_sb = at("c_bias", [128, 32], F32, nxt(128))
    bw_sb = at("c_bw", [1, 48], F32, nxt(192))
    ws_sb = at("c_ws", [1, 1], F32, nxt(32))
    bwb_sb = at("c_bwb", [128, 48], F32, nxt(192))
    bd_sb = at("c_bd", [128, 16 * 128], BF16, nxt(4096))
    stat_sb = at("c_stat", [128, 96], F32, nxt(384))
    am_sb = at("c_am", [128, 8], F32, nxt(32))
    off_rows = nxt(6144)
    rows_sb = at("c_rows", [1, 1536], F32, off_rows)
    off_bca = nxt(2048)
    bca_sb = at("c_bca", [128, 512], F32, off_bca)
    bcb_sb = at("c_bcb", [128, 512], F32, nxt(2048))
    off_tst = nxt(2048)
    tst_sb = at("c_tst", [128, 512], F32, off_tst)
    diag_sb = [at("c_diag0", [128, 8 * 128], BF16, nxt(2048)),
               at("c_diag1", [128, 8 * 128], BF16, off_tst)]
    PB = (o[0] + 63) // 64 * 64
    qstg_sb = [at("c_qstg0", [128, 512], BF16, PB + 160 * KB),
               at("c_qstg1", [128, 512], BF16, PB + 161 * KB)]

    NM, SM, S4c, T4c = 0, 8, 16, 24
    MNEG, TOT, INV = 32, 33, 34

    wkh_sb = [at(f"a_wk{i}", [128, 32 * 256], BF16, PB + i * 16 * KB) for i in range(2)]
    wvh_sb = [at(f"a_wv{i}", [128, 32 * 256], BF16, PB + 32 * KB + i * 16 * KB) for i in range(2)]
    xb_sb = [at(f"a_xb{i}", [128, NFB * 512], BF16, PB + 64 * KB + i * 32 * KB) for i in range(2)]
    kstg_sb = [at(f"a_ks{i}", [128, 2 * 512], BF16, PB + 128 * KB + i * 2 * KB) for i in range(2)]
    vstg_sb = [at(f"a_vs{i}", [128, 4 * 256], BF16, PB + 132 * KB + i * 2 * KB) for i in range(2)]
    qeff_sb = at("b_qeff", [128, 32 * 512], BF16, PB + 0)
    xs_sb = at("b_xs", [128, NFB * 512], F32, PB + 32 * KB)
    kh_sb = [at(f"b_kh{i}", [128, 2 * L], BF16, PB + 96 * KB + i * 16 * KB) for i in range(2)]
    vh_sb = [at(f"b_vh{i}", [128, 32 * HD], BF16, PB + 128 * KB + i * 16 * KB) for i in range(2)]
    p_sb = [at("b_p0", [128, L], BF16, PB + 160 * KB),
            at("b_p1", [128, L], BF16, off_rows)]
    pt_sb = at("b_pt", [128, 32 * 256], BF16, PB + 168 * KB)
    wqh_sb = [at(f"b_wq{i}", [128, 32 * 256], BF16, PB + 128 * KB + i * 16 * KB) for i in range(2)]
    xq_sb = at("b_xq", [128, NFB * 512], BF16, PB + 96 * KB)
    xbf_sb = at("f_xbf", [128, NFB * 512], BF16, PB + 0)
    sq_sb = at("f_sq", [128, NFB * 512], BF16, PB + 96 * KB)
    wfh_sb = [at(f"f_wf{i}", [128, 32 * 256], BF16, PB + 96 * KB + i * 16 * KB) for i in range(2)]
    xn_sb = at("f_xn", [128, NFB * 512], BF16, PB + 128 * KB)
    assert BASE0 + PB + 184 * KB <= nc._sbuf_addr_for_side('right'), (BASE0, PB)

    ps_a = nc.alloc_psum_tensor("ps_a", [128, 1024], F32).ap()
    ps_b = nc.alloc_psum_tensor("ps_b", [128, 1024], F32).ap()
    ps_cd = nc.alloc_psum_tensor("ps_cd", [128, 1024], F32).ap()
    ps_ef = nc.alloc_psum_tensor("ps_ef", [128, 1024], F32).ap()
    ps_c = ps_cd[:, 0:512]
    ps_d = ps_cd[:, 512:1024]
    ps_e = ps_ef[:, 0:512]
    ps_f = ps_ef[:, 512:1024]

    sLD, sCD, sST = PSem("sLD"), PSem("sCD"), PSem("sST")
    sPE, sACT, sDVE = PSem("sPE"), PSem("sACT"), PSem("sDVE")
    bsems = {e: PSem("sB_" + e) for e in ENGS}
    all_sems = [sLD, sCD, sST, sPE, sACT, sDVE]

    pl = Plan()

    def barrier():
        targets = {s.name: s.n for s in all_sems}
        for e in ENGS:
            for s in all_sems:
                pl.wait(e, s, targets[s.name])
        for e in ENGS:
            b = bsems[e]
            b.n += 1
            pl.emit(e, lambda eng, s=b: eng.sem_inc(s.h, 1))
        for e in ENGS:
            for e2 in ENGS:
                pl.wait(e, bsems[e2], bsems[e2].n)

    ZONE = ["x"]

    def _pq(e):
        if PARANOID is True:
            return True
        if isinstance(PARANOID, (set, frozenset, list, tuple)):
            return ZONE[0] in PARANOID
        return False

    def _paranoid(e):
        if not _pq(e):
            return
        for s in all_sems:
            pl.wait(e, s, s.n)

    def dma(e, out, in_, sem=None):
        _paranoid(e)
        sem = sem or (sCD if e == "gpsimd" else sLD)
        sem.n += 16
        pl.emit(e, lambda eng, o_=out, i_=in_, s=sem: eng.dma_start(out=o_, in_=i_).then_inc(s.h, 16))
        return sem.n

    def mm(out, lhsT, rhs, start, stop, inc=False):
        _paranoid("tensor")
        if _pq("tensor"):
            inc = True
        if inc:
            sPE.n += 1
            pl.emit("tensor", lambda eng, o_=out, l_=lhsT, r_=rhs, st=start, sp=stop:
                    eng.matmul(o_, l_, r_, start=st, stop=sp).then_inc(sPE.h, 1))
            return sPE.n
        pl.emit("tensor", lambda eng, o_=out, l_=lhsT, r_=rhs, st=start, sp=stop:
                eng.matmul(o_, l_, r_, start=st, stop=sp))

    def ptrans(out, in_, ident, inc=False):
        _paranoid("tensor")
        if _pq("tensor"):
            inc = True
        if inc:
            sPE.n += 1
            pl.emit("tensor", lambda eng, o_=out, i_=in_, d_=ident:
                    eng.transpose(o_, i_, d_).then_inc(sPE.h, 1))
            return sPE.n
        pl.emit("tensor", lambda eng, o_=out, i_=in_, d_=ident: eng.transpose(o_, i_, d_))

    def act(out, in_, func=AF.Copy, bias=0.0, scale=1.0, accum=None, inc=False):
        _paranoid("scalar")
        if _pq("scalar"):
            inc = True
        if func == AF.Copy and not isinstance(bias, float):
            func = AF.Identity

        def f(eng, o_=out, i_=in_, fu=func, b_=bias, sc=scale, ac=accum):
            if ac is None:
                return eng.activation(o_, i_, fu, bias=b_, scale=sc)
            return eng.activation(o_, i_, fu, bias=b_, scale=sc, accum_out=ac)
        if inc:
            sACT.n += 1
            pl.emit("scalar", lambda eng, f_=f: f_(eng).then_inc(sACT.h, 1))
            return sACT.n
        pl.emit("scalar", f)

    def vop(builder, inc=False):
        _paranoid("vector")
        if _pq("vector"):
            inc = True
        if inc:
            sDVE.n += 1
            pl.emit("vector", lambda eng, b=builder: b(eng).then_inc(sDVE.h, 1))
            return sDVE.n
        pl.emit("vector", builder)

    for dst, src in ((ident_f32[:, :], idf_in), (ident_bf16[:, :], idb_in),
                     (ones_f32[:, :], onf_in), (ones_bf16[:, :], onb_in),
                     (d0g_sb[:, :], d0g_in), (bq_sb[:, :], bq_in),
                     (bk_sb[:, :], bk_in), (bv_sb[:, :], bv_in),
                     (bb_sb[:, :], bb_in), (bw_sb[:, :], bw_in),
                     (ws_sb[:, :], ws_in)):
        dma("sync", dst, src)
    v_consts = sLD.n

    v_xconv = 0
    for i in range(4):
        v_xconv = dma("gpsimd", xbd[2 * i:2 * i + 2], x_sw[2 * i:2 * i + 2])

    pl.wait("tensor", sLD, v_consts)
    v_bwmm = mm(ps_c[:, 0:48], ones_f32[0:1, 0:128], bw_sb[0:1, 0:48], True, True, inc=True)
    pl.wait("scalar", sPE, v_bwmm)
    v_bwb = act(bwb_sb[:, :], ps_c[:, 0:48], inc=True)

    pl.wait("vector", sACT, v_bwb)
    pl.wait("vector", sLD, v_consts)
    v_bd = 0
    for h in range(16):
        g0, g1, g2 = d0g_sb[:, 128:256], d0g_sb[:, 256:384], d0g_sb[:, 384:512]
        d0 = d0g_sb[:, 0:128]
        acc = tst_sb[:, 0:128]
        s0 = bwb_sb[:, 3 * h:3 * h + 1]
        s1 = bwb_sb[:, 3 * h + 1:3 * h + 2]
        s2 = bwb_sb[:, 3 * h + 2:3 * h + 3]
        out = bd_sb[:, h * 128:(h + 1) * 128]
        vop(lambda e, a=acc, g=g0, s=s0: e.tensor_scalar_mul(a, g, s))
        vop(lambda e, a=acc, g=g1, s=s1: e.scalar_tensor_tensor(a, g, s, a, ALU.mult, ALU.add))
        vop(lambda e, a=acc, g=g2, s=s2: e.scalar_tensor_tensor(a, g, s, a, ALU.mult, ALU.add))
        v_bd = vop(lambda e, a=acc, d_=d0, o_=out: e.tensor_add(o_, a, d_), inc=True)

    if dbg:
        pl.wait("sync", sDVE, v_bd)
        dma("sync", dbg_outs["dbg_bd"], bd_sb[:, :], sST)

    ev_ps = {"c": v_bwb, "d": 0, "e": 0, "f": 0}
    st_kstg = [0, 0]
    st_vstg = [0, 0]
    v_xb = [0, 0]
    pe_xb_done = [0, 0]
    v_w = [0, 0]
    pe_w_done = [0, 0]

    def load_xchunk(ci):
        par = ci % 2
        lc = ci % 8
        pl.wait("sync", sPE, pe_xb_done[par])
        pl.wait("sync", sCD, v_xconv)
        v_xb[par] = dma("sync", xb_sb[par][:, :], xbd[lc].rearrange("p fb l -> p (fb l)"))

    def load_wpair(g):
        wpar = g % 2
        pl.wait("gpsimd", sPE, pe_w_done[wpar])
        dma("gpsimd", wkh_sb[wpar][:, :], wk_sw[g])
        v_w[wpar] = dma("gpsimd", wvh_sb[wpar][:, :], wv_sw[g])

    if upto != "none":
        load_xchunk(0)
        load_xchunk(1)
        load_wpair(0)
        load_wpair(1)
        for g in range(16):
            wpar = g % 2
            for lc in range(8):
                ci = g * 8 + lc
                par = ci % 2
                pl.wait("tensor", sCD, v_w[wpar])
                pl.wait("tensor", sLD, v_xb[par])
                for ob in range(2):
                    bank = ("c", "d")[ob]
                    ps = (ps_c, ps_d)[ob]
                    pl.wait("tensor", sACT, ev_ps[bank])
                    for ib in range(32):
                        vv = mm(ps[:, 0:512],
                                wkh_sb[wpar][:, ib * 256 + ob * 128: ib * 256 + ob * 128 + 128],
                                xb_sb[par][:, ib * 512: ib * 512 + 512],
                                ib == 0, ib == 31, inc=(ib == 31))
                    pl.wait("scalar", sPE, vv)
                    pl.wait("scalar", sST, st_kstg[par])
                    ev_ps[bank] = act(kstg_sb[par][:, ob * 512:(ob + 1) * 512], ps[:, 0:512],
                                      AF.Copy, bias=bk_sb[:, g * 2 + ob: g * 2 + ob + 1],
                                      inc=True)
                pl.wait("sync", sACT, ev_ps["d"])
                st_kstg[par] = dma(
                    "sync",
                    kd[g * 2:(g + 1) * 2, :, lc * 512:(lc + 1) * 512].rearrange("b p l -> p b l"),
                    kstg_sb[par][:, :].rearrange("p (b l) -> p b l", b=2), sST)
                for lt in range(4):
                    bank = ("e", "f")[lt % 2]
                    ps = (ps_e, ps_f)[lt % 2]
                    pl.wait("tensor", sACT, ev_ps[bank])
                    for ib in range(32):
                        vv = mm(ps[:, 0:256],
                                xb_sb[par][:, ib * 512 + lt * 128: ib * 512 + lt * 128 + 128],
                                wvh_sb[wpar][:, ib * 256: ib * 256 + 256],
                                ib == 0, ib == 31, inc=(ib == 31))
                    pl.wait("scalar", sPE, vv)
                    pl.wait("scalar", sST, st_vstg[par])
                    ev_ps[bank] = act(vstg_sb[par][:, lt * 256:(lt + 1) * 256], ps[:, 0:256],
                                      inc=True)
                pl.wait("sync", sACT, ev_ps[("e", "f")[3 % 2]])
                st_vstg[par] = dma(
                    "sync",
                    vd[g, lc * 4:(lc + 1) * 4].rearrange("mt p f -> p mt f"),
                    vstg_sb[par][:, :].rearrange("p (lt f) -> p lt f", lt=4), sST)
                pe_xb_done[par] = sPE.n
                if ci + 2 < 128:
                    load_xchunk(ci + 2)
            pe_w_done[wpar] = sPE.n
            if g + 2 < 16:
                load_wpair(g + 2)

    if dbg:
        pl.wait("sync", sST, sST.n)
        dma("sync", dbg_outs["dbg_xbd"], xbd, sST)
        dma("sync", dbg_outs["dbg_k"], kd, sST)
        dma("sync", dbg_outs["dbg_v"], vd, sST)

    barrier()
    if upto == "kv":
        dma("gpsimd", outT, xbd[0:2])
        for s in all_sems:
            pl.wait("sync", s, s.n)

    def half_program(hf):
        v_xq = dma("sync", xq_sb[:, :], xbd[hf].rearrange("p fb l -> p (fb l)"))
        v_wq = [0, 0]
        pe_wq_done = [0, 0]
        q_ev = [0, 0]
        qe_ev = {"c": 0, "d": 0}
        bl_mm = [0, 0]
        qe_ready = 0

        def load_wq(g):
            wpar = g % 2
            pl.wait("gpsimd", sPE, pe_wq_done[wpar])
            v_wq[wpar] = dma("gpsimd", wqh_sb[wpar][:, :], wq_sw[g])

        load_wq(0)
        load_wq(1)
        for g in range(16):
            wpar = g % 2
            pl.wait("tensor", sCD, v_wq[wpar])
            pl.wait("tensor", sLD, v_xq)
            for ob in range(2):
                obg = g * 2 + ob
                ps = (ps_a[:, 0:512], ps_a[:, 512:1024])[ob]
                pl.wait("tensor", sACT, q_ev[ob])
                for ib in range(32):
                    vq = mm(ps,
                            wqh_sb[wpar][:, ib * 256 + ob * 128: ib * 256 + ob * 128 + 128],
                            xq_sb[:, ib * 512: ib * 512 + 512],
                            ib == 0, ib == 31, inc=(ib == 31))
                pl.wait("scalar", sPE, vq)
                pl.wait("scalar", sPE, bl_mm[ob])
                q_ev[ob] = act(qstg_sb[ob][:, :], ps, AF.Copy,
                               bias=bq_sb[:, obg:obg + 1], inc=True)
            for ob in range(2):
                obg = g * 2 + ob
                h = obg // 2
                cb = ("c", "d")[ob]
                psb = (ps_c, ps_d)[ob]
                pl.wait("tensor", sACT, q_ev[ob])
                pl.wait("tensor", sACT, qe_ev[cb])
                bl_mm[ob] = mm(psb[:, 0:512], bd_sb[:, h * 128:(h + 1) * 128],
                               qstg_sb[ob][:, :], True, True, inc=True)
                pl.wait("scalar", sPE, bl_mm[ob])
                qe_ev[cb] = act(qeff_sb[:, obg * 512:(obg + 1) * 512], psb[:, 0:512], inc=True)
                qe_ready = qe_ev[cb]
            pe_wq_done[wpar] = sPE.n
            if g + 2 < 16:
                load_wq(g + 2)
        if dbg:
            pl.wait("sync", sACT, qe_ready)
            dma("sync", dbg_outs["dbg_qeff"][hf], qeff_sb[:, :], sST)

        v_xs = dma("sync", xs_sb[:, :], x_sw[hf].rearrange("p fb l -> p (fb l)"))
        barrier()
        if upto == "q":
            return

        v_kh = [0, 0]
        v_vh = [0, 0]
        kh_free = [0, 0]
        s_bufs = (ps_a[:, 0:512], ps_a[:, 512:1024], ps_b[:, 0:512], ps_b[:, 512:1024])
        s_bank_ev = [0, 0, 0, 0]
        s_rot = [0]
        bank_act = {"c": 0, "d": 0, "e": 0, "f": 0}
        bank_dve = {"c": 0, "d": 0, "e": 0, "f": 0}
        PSB = {"c": ps_c, "d": ps_d, "e": ps_e, "f": ps_f}
        v_pt_last = [0, 0]
        v_av_last = 0
        ev_p_qt = [0, 0]
        st_done = [0, 0]
        av_ready_a = [0, 0]
        av_ready_d = [0, 0]
        pt4 = pt_sb[:, :].rearrange("p (q mtl) -> p q mtl", q=2)
        v_kh[0] = dma("sync", kh_sb[0][:, :].rearrange("p (b m) -> p b m", b=2),
                      kd[0:2].rearrange("b p m -> p b m"))
        v_vh[0] = dma("sync", vh_sb[0][:, :].rearrange("p (mt f) -> p mt f", mt=32),
                      vd[0].rearrange("mt p f -> p mt f"))
        pending = None

        for h in range(NH):
            par = h % 2

            def S_qt(qt):
                ZONE[0] = "S"
                pq = qt % 2
                sb_off = 48 * pq
                for ck in range(8):
                    sb = s_rot[0] % 4
                    s_rot[0] += 1
                    ps = s_bufs[sb]
                    pl.wait("tensor", sACT, s_bank_ev[sb])
                    for fb2 in range(2):
                        vS = mm(ps,
                                qeff_sb[:, (2 * h + fb2) * 512 + qt * 128:
                                        (2 * h + fb2) * 512 + qt * 128 + 128],
                                kh_sb[par][:, fb2 * L + ck * 512:
                                           fb2 * L + ck * 512 + 512],
                                fb2 == 0, fb2 == 1, inc=(fb2 == 1))
                    pl.wait("vector", sPE, vS)
                    v_nm = vop(lambda e, ps_=ps, c_=ck + sb_off:
                               e.reduce_max(stat_sb[:, NM + c_:NM + c_ + 1], ps_,
                                            axis=AX.X, negate=True), inc=True)
                    pl.wait("scalar", sDVE, v_nm)
                    pl.wait("scalar", sPE, v_pt_last[pq])
                    s_bank_ev[sb] = act(p_sb[pq][:, ck * 512:(ck + 1) * 512], ps,
                                        AF.Exp,
                                        bias=stat_sb[:, NM + sb_off + ck:NM + sb_off + ck + 1],
                                        scale=1.0,
                                        accum=stat_sb[:, SM + sb_off + ck:SM + sb_off + ck + 1],
                                        inc=True)
                ev_p_qt[pq] = s_bank_ev[(s_rot[0] - 1) % 4]

            def ST_qt(qt):
                ZONE[0] = "stats"
                pq = qt % 2
                c0 = 48 * pq
                pl.wait("vector", sACT, ev_p_qt[pq])
                pl.wait("vector", sPE, v_pt_last[pq])
                v1 = vop(lambda e: e.tensor_reduce(stat_sb[:, c0 + MNEG:c0 + MNEG + 1],
                                                   stat_sb[:, c0 + NM:c0 + NM + 8],
                                                   AX.X, ALU.min), inc=True)
                pl.wait("scalar", sDVE, v1)
                v3 = act(stat_sb[:, c0 + S4c:c0 + S4c + 8], stat_sb[:, c0 + NM:c0 + NM + 8],
                         AF.Exp, bias=stat_sb[:, c0 + MNEG:c0 + MNEG + 1], scale=-1.0, inc=True)
                pl.wait("vector", sACT, v3)
                v4 = vop(lambda e: e.scalar_tensor_tensor(
                    stat_sb[:, c0 + T4c:c0 + T4c + 8], stat_sb[:, c0 + S4c:c0 + S4c + 8],
                    1.0, stat_sb[:, c0 + SM:c0 + SM + 8], ALU.mult, ALU.mult,
                    accum_out=stat_sb[:, c0 + TOT:c0 + TOT + 1]), inc=True)
                pl.wait("vector", sDVE, v4)
                v5 = vop(lambda e: e.reciprocal(stat_sb[:, c0 + INV:c0 + INV + 1],
                                                stat_sb[:, c0 + TOT:c0 + TOT + 1]), inc=True)
                pl.wait("vector", sDVE, v5)
                vd_ = 0
                for ck in range(8):
                    vd_ = vop(lambda e, c_=ck:
                              e.tensor_scalar(diag_sb[pq][:, c_ * 128:(c_ + 1) * 128],
                                              ident_bf16[:, :],
                                              stat_sb[:, c0 + S4c + c_:c0 + S4c + c_ + 1],
                                              stat_sb[:, c0 + INV:c0 + INV + 1],
                                              ALU.mult, ALU.mult), inc=True)
                st_done[pq] = vd_

            def PT_qt(qt):
                ZONE[0] = "PT"
                pq = qt % 2
                qtl = qt % 2
                pl.wait("tensor", sDVE, st_done[pq])
                pl.wait("tensor", sACT, ev_p_qt[pq])
                last_a, last_d = 0, 0
                for r in range(8):
                    bk = ("c", "d", "e", "f")[r % 4]
                    ps = PSB[bk]
                    pl.wait("tensor", sACT, bank_act[bk])
                    pl.wait("tensor", sDVE, bank_dve[bk])
                    for j in range(4):
                        mt = 4 * r + j
                        vPT = mm(ps[:, j * 128:(j + 1) * 128],
                                 p_sb[pq][:, mt * 128:(mt + 1) * 128],
                                 diag_sb[pq][:, (mt // 4) * 128:(mt // 4) * 128 + 128],
                                 True, True, inc=(j == 3))
                    v_pt_last[pq] = vPT
                    dst = pt_sb[:, qtl * 4096 + r * 512: qtl * 4096 + (r + 1) * 512]
                    if r % 2 == 0:
                        pl.wait("scalar", sPE, vPT)
                        pl.wait("scalar", sPE, v_av_last)
                        last_a = act(dst, ps[:, :], inc=True)
                        bank_act[bk] = last_a
                    else:
                        pl.wait("vector", sPE, vPT)
                        pl.wait("vector", sPE, v_av_last)
                        last_d = vop(lambda e, d_=dst, ps_=ps: e.tensor_copy(d_, ps_[:, :]),
                                     inc=True)
                        bank_dve[bk] = last_d
                av_ready_a[qt // 2] = last_a
                av_ready_d[qt // 2] = last_d

            def AV_qp(qp, hh, pp):
                nonlocal v_av_last
                ZONE[0] = "AV"
                pl.wait("tensor", sACT, av_ready_a[qp])
                pl.wait("tensor", sDVE, av_ready_d[qp])
                pl.wait("tensor", sLD, v_vh[pp])
                for fb2 in range(2):
                    bk = ("e", "f")[fb2]
                    ps = PSB[bk]
                    pl.wait("tensor", sDVE, bank_dve[bk])
                    pl.wait("tensor", sACT, bank_act[bk])
                    for mt in range(32):
                        vAV = mm(ps[:, 0:256],
                                 vh_sb[pp][:, mt * HD + fb2 * 128: mt * HD + fb2 * 128 + 128],
                                 pt4[:, :, mt * 128:(mt + 1) * 128],
                                 mt == 0, mt == 31, inc=(mt == 31))
                    v_av_last = vAV
                    fb = 2 * hh + fb2
                    xsl = xs_sb[:, fb * 512 + qp * 256: fb * 512 + qp * 256 + 256]
                    pl.wait("vector", sPE, vAV)
                    pl.wait("vector", sLD, v_xs)
                    bank_dve[bk] = vop(
                        lambda e, ps_=ps, x_=xsl, b_=bv_sb[:, fb:fb + 1]:
                        e.scalar_tensor_tensor(x_, ps_[:, 0:256], b_, x_, ALU.add, ALU.add),
                        inc=True)

            pl.wait("tensor", sLD, v_kh[par])
            S_qt(0)
            S_qt(1)
            if pending is not None:
                AV_qp(*pending)
                pending = None
            if h + 1 < NH:
                np_ = (h + 1) % 2
                kh_free[np_] = sPE.n
                pl.wait("sync", sPE, kh_free[np_])
                v_kh[np_] = dma("sync", kh_sb[np_][:, :].rearrange("p (b m) -> p b m", b=2),
                                kd[2 * (h + 1):2 * (h + 1) + 2].rearrange("b p m -> p b m"))
                v_vh[np_] = dma("sync", vh_sb[np_][:, :].rearrange("p (mt f) -> p mt f", mt=32),
                                vd[h + 1].rearrange("mt p f -> p mt f"))
            ST_qt(0)
            ST_qt(1)
            PT_qt(0)
            PT_qt(1)
            S_qt(2)
            S_qt(3)
            AV_qp(0, h, par)
            ST_qt(2)
            ST_qt(3)
            PT_qt(2)
            PT_qt(3)
            pending = (1, h, par)
            ZONE[0] = "x"
        AV_qp(*pending)
        pending = None
        if dbg:
            pl.wait("sync", sDVE, sDVE.n)
            dma("sync", dbg_outs["dbg_attn"][hf], xs_sb[:, :], sST)
        barrier()
        if upto == "attn":
            pl.wait("sync", sDVE, sDVE.n)
            dma("sync", outT[hf], xs_sb[:, :].rearrange("p (fb l) -> p fb l", fb=NFB), sST)
            barrier()
            return

        def layer_norm(dump_key):
            v_cp = 0
            for fb in range(NFB):
                v_cp = vop(lambda e, f_=fb: e.tensor_copy(
                    xbf_sb[:, f_ * 512:(f_ + 1) * 512], xs_sb[:, f_ * 512:(f_ + 1) * 512]),
                    inc=(fb == NFB - 1))
            v_sq = 0
            for fb in range(NFB):
                v_sq = vop(lambda e, f_=fb: e.tensor_mul(
                    sq_sb[:, f_ * 512:(f_ + 1) * 512], xbf_sb[:, f_ * 512:(f_ + 1) * 512],
                    xbf_sb[:, f_ * 512:(f_ + 1) * 512]), inc=(fb == NFB - 1))
            pl.wait("tensor", sDVE, v_cp)
            for fb in range(NFB):
                vsm = mm(ps_e[0:1, 0:512], ones_bf16[:, 0:1],
                         xbf_sb[:, fb * 512:(fb + 1) * 512], fb == 0, fb == NFB - 1,
                         inc=(fb == NFB - 1))
            pl.wait("tensor", sDVE, v_sq)
            for fb in range(NFB):
                vsq = mm(ps_f[0:1, 0:512], ones_bf16[:, 0:1],
                         sq_sb[:, fb * 512:(fb + 1) * 512], fb == 0, fb == NFB - 1,
                         inc=(fb == NFB - 1))
            pl.wait("scalar", sPE, vsq)
            act(rows_sb[0:1, 0:512], ps_e[0:1, 0:512])
            v_rows = act(rows_sb[0:1, 512:1024], ps_f[0:1, 0:512], inc=True)
            pl.wait("vector", sACT, v_rows)
            vop(lambda e: e.tensor_scalar_mul(rows_sb[0:1, 0:512], rows_sb[0:1, 0:512], 1.0 / D))
            vop(lambda e: e.tensor_scalar_mul(rows_sb[0:1, 512:1024], rows_sb[0:1, 512:1024], 1.0 / D))
            vop(lambda e: e.tensor_mul(rows_sb[0:1, 1024:1536], rows_sb[0:1, 0:512],
                                       rows_sb[0:1, 0:512]))
            vop(lambda e: e.tensor_sub(rows_sb[0:1, 512:1024], rows_sb[0:1, 512:1024],
                                       rows_sb[0:1, 1024:1536]))
            v_var = vop(lambda e: e.tensor_scalar_add(rows_sb[0:1, 512:1024],
                                                      rows_sb[0:1, 512:1024], LN_EPS), inc=True)
            pl.wait("scalar", sDVE, v_var)
            act(rows_sb[0:1, 512:1024], rows_sb[0:1, 512:1024], AF.Ln, bias=0.0, scale=1.0)
            v_rstd = act(rows_sb[0:1, 512:1024], rows_sb[0:1, 512:1024], AF.Exp,
                         bias=0.0, scale=-0.5, inc=True)
            pl.wait("vector", sACT, v_rstd)
            vop(lambda e: e.tensor_mul(rows_sb[0:1, 0:512], rows_sb[0:1, 0:512],
                                       rows_sb[0:1, 512:1024]))
            v_mr = vop(lambda e: e.tensor_scalar_mul(rows_sb[0:1, 0:512],
                                                     rows_sb[0:1, 0:512], -1.0), inc=True)
            pl.wait("tensor", sDVE, v_mr)
            mm(ps_c[:, 0:512], ones_f32[0:1, 0:128], rows_sb[0:1, 512:1024], True, True, inc=True)
            v_b2 = mm(ps_d[:, 0:512], ones_f32[0:1, 0:128], rows_sb[0:1, 0:512], True, True, inc=True)
            pl.wait("scalar", sPE, v_b2)
            act(bca_sb[:, :], ps_c[:, 0:512])
            v_bc = act(bcb_sb[:, :], ps_d[:, 0:512], inc=True)
            pl.wait("vector", sACT, v_bc)
            v_ln = 0
            for fb in range(NFB):
                vop(lambda e, f_=fb: e.tensor_mul(xs_sb[:, f_ * 512:(f_ + 1) * 512],
                                                  xs_sb[:, f_ * 512:(f_ + 1) * 512],
                                                  bca_sb[:, :]))
                v_ln = vop(lambda e, f_=fb: e.tensor_add(xs_sb[:, f_ * 512:(f_ + 1) * 512],
                                                         xs_sb[:, f_ * 512:(f_ + 1) * 512],
                                                         bcb_sb[:, :]), inc=(fb == NFB - 1))
            if dbg and dump_key is not None:
                pl.wait("sync", sDVE, v_ln)
                dma("sync", dbg_outs[dump_key][hf], xs_sb[:, :], sST)
            return v_ln

        layer_norm("dbg_ln1")
        barrier()
        if upto == "ln1":
            pl.wait("sync", sDVE, sDVE.n)
            dma("sync", outT[hf], xs_sb[:, :].rearrange("p (fb l) -> p fb l", fb=NFB), sST)
            barrier()
            return

        tr_dve = [0, 0]
        v_am = 0
        for lt in range(4):
            for r in range(4):
                sbk = r % 2
                ps = (ps_a, ps_b)[sbk]
                pl.wait("tensor", sDVE, tr_dve[sbk])
                for j in range(8):
                    fb = r * 8 + j
                    vT = ptrans(ps[:, j * 128:(j + 1) * 128],
                                xs_sb[:, fb * 512 + lt * 128: fb * 512 + lt * 128 + 128],
                                ident_f32[:, :], inc=(j == 7))
                pl.wait("vector", sPE, vT)
                tr_dve[sbk] = vop(lambda e, ps_=ps, r_=r:
                                  e.reduce_max(stat_sb[:, NM + r_:NM + r_ + 1], ps_[:, :],
                                               axis=AX.X, apply_absolute_value=True), inc=True)
            pl.wait("vector", sDVE, tr_dve[0])
            pl.wait("vector", sDVE, tr_dve[1])
            vop(lambda e, l_=lt: e.tensor_reduce(am_sb[:, l_:l_ + 1],
                                                 stat_sb[:, NM:NM + 4], AX.X, ALU.max))
            v_clip = vop(lambda e, l_=lt: e.tensor_scalar_max(am_sb[:, l_:l_ + 1],
                                                              am_sb[:, l_:l_ + 1], FFN_EPS),
                         inc=True)
            pl.wait("vector", sDVE, v_clip)
            v_am = vop(lambda e, l_=lt: e.reciprocal(am_sb[:, 4 + l_:5 + l_],
                                                     am_sb[:, l_:l_ + 1]), inc=True)
        pl.wait("tensor", sDVE, v_am)
        for lt in range(4):
            ptrans(ps_e[0:1, lt * 128:(lt + 1) * 128], am_sb[:, lt:lt + 1], ident_f32[:, :])
            vT2 = ptrans(ps_f[0:1, lt * 128:(lt + 1) * 128], am_sb[:, 4 + lt:5 + lt],
                         ident_f32[:, :], inc=(lt == 3))
        pl.wait("scalar", sPE, vT2)
        act(rows_sb[0:1, 0:512], ps_e[0:1, 0:512])
        v_rows2 = act(rows_sb[0:1, 512:1024], ps_f[0:1, 0:512], inc=True)
        pl.wait("vector", sACT, v_rows2)
        v_amw = vop(lambda e: e.tensor_scalar_mul(rows_sb[0:1, 0:512], rows_sb[0:1, 0:512],
                                                  ws_sb[0:1, 0:1]), inc=True)
        pl.wait("tensor", sDVE, v_amw)
        mm(ps_c[:, 0:512], ones_f32[0:1, 0:128], rows_sb[0:1, 512:1024], True, True, inc=True)
        v_b2 = mm(ps_d[:, 0:512], ones_f32[0:1, 0:128], rows_sb[0:1, 0:512], True, True, inc=True)
        pl.wait("scalar", sPE, v_b2)
        act(bca_sb[:, :], ps_c[:, 0:512])
        v_bc2 = act(bcb_sb[:, :], ps_d[:, 0:512], inc=True)
        pl.wait("vector", sACT, v_bc2)
        v_xn = 0
        for fb in range(NFB):
            v_xn = vop(lambda e, f_=fb: e.tensor_mul(xn_sb[:, f_ * 512:(f_ + 1) * 512],
                                                     xs_sb[:, f_ * 512:(f_ + 1) * 512],
                                                     bca_sb[:, :]), inc=(fb == NFB - 1))
        v_wf = [0, 0]
        v_wfc = [0, 0]
        pe_wf_done = [0, 0]
        ffn_ev = {"c": 0, "d": 0}

        def load_wf(g):
            wpar = g % 2
            pl.wait("gpsimd", sPE, pe_wf_done[wpar])
            v_wf[wpar] = dma("gpsimd", wfh_sb[wpar][:, :], wf_sw[g])
            pl.wait("vector", sCD, v_wf[wpar])
            pl.wait("vector", sPE, pe_wf_done[wpar])
            v_wfc[wpar] = vop(lambda e, w=wfh_sb[wpar]:
                              e.tensor_scalar_add(w[:, :], w[:, :], -1.5), inc=True)

        load_wf(0)
        load_wf(1)
        for g in range(16):
            wpar = g % 2
            pl.wait("tensor", sDVE, v_wfc[wpar])
            pl.wait("tensor", sDVE, v_xn)
            for ob in range(2):
                obg = g * 2 + ob
                cb = ("c", "d")[ob]
                ps = (ps_c, ps_d)[ob]
                pl.wait("tensor", sDVE, ffn_ev[cb])
                for ib in range(32):
                    vF = mm(ps[:, 0:512],
                            wfh_sb[wpar][:, ib * 256 + ob * 128: ib * 256 + ob * 128 + 128],
                            xn_sb[:, ib * 512: ib * 512 + 512],
                            ib == 0, ib == 31, inc=(ib == 31))
                pl.wait("vector", sPE, vF)
                vop(lambda e, ps_=ps: e.tensor_mul(tst_sb[:, :], ps_[:, 0:512], bcb_sb[:, :]))
                xsl = xs_sb[:, obg * 512:(obg + 1) * 512]
                ffn_ev[cb] = vop(
                    lambda e, x_=xsl, b_=bb_sb[:, obg:obg + 1]:
                    e.scalar_tensor_tensor(x_, tst_sb[:, :], b_, x_, ALU.add, ALU.add),
                    inc=True)
            pe_wf_done[wpar] = sPE.n
            if g + 2 < 16:
                load_wf(g + 2)
        if dbg:
            pl.wait("sync", sDVE, sDVE.n)
            dma("sync", dbg_outs["dbg_ffn"][hf], xs_sb[:, :], sST)
        barrier()
        if upto == "ffn":
            pl.wait("sync", sDVE, sDVE.n)
            dma("sync", outT[hf], xs_sb[:, :].rearrange("p (fb l) -> p fb l", fb=NFB), sST)
            barrier()
            return

        v_done = layer_norm(None)
        pl.wait("sync", sDVE, v_done)
        dma("sync", outT[hf], xs_sb[:, :].rearrange("p (fb l) -> p fb l", fb=NFB), sST)
        barrier()

    if upto != "kv":
        for hf in range(2):
            half_program(hf)

    for s in all_sems:
        pl.wait("sync", s, s.n)

    import contextlib
    st = contextlib.ExitStack()
    for s in all_sems + list(bsems.values()):
        s.h = st.enter_context(nc.semaphore(s.name))
    with st:
        with nc.Block() as blk:
            blk.tensor(lambda eng: [f(eng) for f in pl.q["tensor"]])
            blk.scalar(lambda eng: [f(eng) for f in pl.q["scalar"]])
            blk.vector(lambda eng: [f(eng) for f in pl.q["vector"]])
            blk.gpsimd(lambda eng: [f(eng) for f in pl.q["gpsimd"]])
            blk.sync(lambda eng: [f(eng) for f in pl.q["sync"]])
    return nc


def _prep_inputs(L0_attn_x, L0_attn_Wq, L0_attn_bq, L0_attn_Wk, L0_attn_bk,
                 L0_attn_Wv, L0_attn_bv, L0_attn_bias_weight,
                 L0_ffn_weight_2bit, L0_ffn_weight_scale, L0_ffn_bias):
    consts = _host_consts()
    wq_sw = _swizzle_w(np.ascontiguousarray(np.asarray(L0_attn_Wq).T))
    wk_sw = _swizzle_w(np.ascontiguousarray(np.asarray(L0_attn_Wk).T))
    wv_sw = _swizzle_w(np.ascontiguousarray(np.asarray(L0_attn_Wv).T))
    wf_sw = _swizzle_w(np.ascontiguousarray(np.asarray(L0_ffn_weight_2bit).T))
    bq_r = np.ascontiguousarray(np.asarray(L0_attn_bq).reshape(32, 128).T)
    bk_r = np.ascontiguousarray(np.asarray(L0_attn_bk).reshape(32, 128).T)
    bv_r = np.ascontiguousarray(np.asarray(L0_attn_bv).reshape(32, 128).T)
    bias_r = np.ascontiguousarray(np.asarray(L0_ffn_bias).reshape(32, 128).T)
    bw_flat = np.ascontiguousarray(np.asarray(L0_attn_bias_weight).reshape(1, 48))
    wscale = np.ascontiguousarray(np.asarray(L0_ffn_weight_scale).reshape(1, 1))
    x = np.asarray(L0_attn_x)
    in_maps, perms = [], []
    xT = [np.ascontiguousarray(x[b].T) for b in range(2)]
    for c in range(8):
        b, q = c // 4, c % 4
        perm = [2 * q, 2 * q + 1] + [j for j in range(8) if j not in (2 * q, 2 * q + 1)]
        perms.append(perm)
        m = dict(x_sw=_swizzle_x(xT[b], perm), wq_sw=wq_sw, wk_sw=wk_sw,
                 wv_sw=wv_sw, wf_sw=wf_sw, bq_r=bq_r, bk_r=bk_r, bv_r=bv_r,
                 bias_r=bias_r, bw_flat=bw_flat, wscale=wscale, d0g=consts["d0g"],
                 ident_f32=consts["ident_f32"], ident_bf16=consts["ident_bf16"],
                 ones_f32=consts["ones_f32"], ones_bf16=consts["ones_bf16"])
        in_maps.append(m)
    return in_maps, perms


def kernel(**inputs):
    in_maps, _ = _prep_inputs(**inputs)
    nc = build_program(dbg=False, upto="full")
    res = run_bass_kernel_spmd(nc, in_maps, core_ids=list(range(8)))
    out = np.empty((2, 4096, 4096), np.float32)
    for c in range(8):
        b, q = c // 4, c % 4
        a = np.asarray(res.results[c]["outT"])
        out[b, q * 1024:(q + 1) * 1024, :] = a.transpose(0, 3, 2, 1).reshape(1024, 4096)
    return out
